# version 06
# baseline (speedup 1.0000x reference)
import numpy as np
import concourse.bass as bass
import concourse.mybir as mybir
from concourse.bass_utils import run_bass_kernel_spmd
from contextlib import ExitStack

F32 = mybir.dt.float32
BF16 = mybir.dt.bfloat16
AF = mybir.ActivationFunctionType
ALU = mybir.AluOpType
AX = mybir.AxisListType

T, D = 4096, 1024
NT = T // 128
ND = D // 128
NH, HD = 16, 64
CH = 512
NCH = T // CH
IFF = 2816
NIT = IFF // 128

ENGS = ("sync", "vector", "scalar", "tensor", "gpsimd")


STRICT = [False]
PARANOID = [True]


class Pr:
    def __init__(self, nc, es):
        self.strict = STRICT[0]
        self.paranoid = PARANOID[0]
        self.nc = nc
        self.es = es
        self.sem = {e: es.enter_context(nc.semaphore(f"sem_{e}")) for e in ENGS}
        self.thunks = {e: [] for e in ENGS}
        self.cnt = {e: 0 for e in ENGS}
        self.waited = {e: {s: 0 for s in ENGS} for e in ENGS}

    def _prune(self, eng, waits, allow_self=False):
        merged = {}
        for (we, wv) in waits:
            if we == eng and not allow_self:
                continue
            if wv > merged.get(we, 0):
                merged[we] = wv
        out = []
        for we, wv in merged.items():
            if wv > self.waited[eng][we]:
                out.append((we, wv))
                self.waited[eng][we] = wv
        return out

    def op(self, eng, opname, *args, waits=(), dma=False, **kwargs):
        inc = 16 if dma else 1
        if self.strict:
            waits = [(e, self.cnt[e]) for e in ENGS if self.cnt[e] > 0]
        wl = self._prune(eng, waits)
        post_drain = (self.paranoid and eng in ("vector", "scalar", "gpsimd")
                      and opname != "dma_start")
        sems = self.sem

        def th(engh, _a=args, _k=kwargs, _wl=wl, _op=opname, _s=sems[eng], _inc=inc,
               _pd=post_drain):
            for (we, wv) in _wl:
                engh.wait_ge(sems[we], wv)
            if _pd:
                getattr(engh, _op)(*_a, **_k)
                engh.drain()
                engh.sem_inc(_s, _inc)
            else:
                getattr(engh, _op)(*_a, **_k).then_inc(_s, _inc)
        self.thunks[eng].append(th)
        self.cnt[eng] += inc
        return (eng, self.cnt[eng])

    def wait_only(self, eng, waits):
        wl = self._prune(eng, waits)
        if not wl:
            return
        sems = self.sem

        def th(engh, _wl=wl):
            for (we, wv) in _wl:
                engh.wait_ge(sems[we], wv)
        self.thunks[eng].append(th)

    def barrier(self):
        snap = [(e, self.cnt[e]) for e in ENGS if self.cnt[e] > 0]
        for e in ENGS:
            self.wait_only(e, snap)

    def tok(self, eng):
        return (eng, self.cnt[eng])

    def all_toks(self):
        return [(e, self.cnt[e]) for e in ENGS if self.cnt[e] > 0]

    def finalize(self):
        nc = self.nc
        with nc.Block() as block:
            for e in ENGS:
                if not self.thunks[e]:
                    continue

                def mk(ename):
                    def run(engh):
                        for th in self.thunks[ename]:
                            th(engh)
                        if ename == "gpsimd":
                            engh.wait_ge(self.sem["gpsimd"], self.cnt["gpsimd"])
                    return run
                getattr(block, e)(mk(e))



def stage_weight_bf16(pr, sb, es, dram_ap, ntiles, freew, name, stg=None):
    nc = pr.nc
    if stg is None:
        stg = [es.enter_context(nc.sbuf_tensor(f"{name}_stg{i}_{pr.cnt['sync']}", [128, freew], F32))
               for i in range(2)]
    out = [sb(f"{name}_bf{k}", [128, freew], BF16) for k in range(ntiles)]
    casts = [pr.tok("vector"), pr.tok("vector")]
    for k in range(ntiles):
        b = k % 2
        ld = pr.op("sync", "dma_start", waits=[casts[b]],
                   out=stg[b][:, 0:freew], in_=dram_ap[k * 128:(k + 1) * 128, :], dma=True)
        casts[b] = pr.op("vector", "tensor_copy", out[k][:], stg[b][:, 0:freew], waits=[ld])
    return out


def bcast_row(pr, ps_bc, dst_sb, row_ap, width, ones1, waits=()):
    toks = list(waits)
    for c0 in range(0, width, 512):
        c1 = min(c0 + 512, width)
        t = pr.op("tensor", "matmul", ps_bc[:, 0:c1 - c0], ones1[0:1, :],
                  row_ap[0:1, c0:c1], start=True, stop=True,
                  waits=toks + [pr.tok("vector")])
        t2 = pr.op("vector", "tensor_copy", dst_sb[:, c0:c1], ps_bc[:, 0:c1 - c0], waits=[t])
        toks = [t2]
    return toks[0]


def layernorm_tile(pr, out_ap, x_ap, scr, r1, r2, musq, eps_ap, gain_b=None, bias_b=None,
                   waits=()):
    w = list(waits)
    t1 = pr.op("vector", "tensor_reduce", r1[:], x_ap, AX.X, ALU.add, waits=w)
    t2 = pr.op("scalar", "activation", scr[:], x_ap, AF.Square,
               waits=w + [pr.tok("vector")])
    t3 = pr.op("vector", "tensor_reduce", r2[:], scr[:], AX.X, ALU.add, waits=[t2])
    t4 = pr.op("vector", "tensor_scalar", r1[:], r1[:], -1.0 / 1024, None, ALU.mult, waits=[t1])
    t5 = pr.op("vector", "tensor_tensor", musq[:], r1[:], r1[:], ALU.mult, waits=[t4])
    t6 = pr.op("vector", "scalar_tensor_tensor", r2[:], r2[:], 1.0 / 1024, musq[:],
               ALU.mult, ALU.subtract, waits=[t3, t5])
    t7 = pr.op("scalar", "activation", r2[:], r2[:], AF.Ln, bias=eps_ap[:, 0:1], waits=[t6])
    t8 = pr.op("scalar", "activation", r2[:], r2[:], AF.Exp, scale=-0.5, waits=[t7])
    dst = out_ap if gain_b is None else scr[:]
    t9 = pr.op("vector", "tensor_scalar", dst, x_ap, r1[:, 0:1], r2[:, 0:1],
               ALU.add, ALU.mult, waits=[t8])
    if gain_b is not None:
        t10 = pr.op("vector", "tensor_tensor", scr[:], scr[:], gain_b[:], ALU.mult, waits=[t9])
        t9 = pr.op("vector", "tensor_tensor", out_ap, scr[:], bias_b[:], ALU.add, waits=[t10])
    return t9


def fp4_quant(pr, out_bf_ap, v_ap, nb, scr, waits=()):
    F = nb * 16
    s, rs, xn, l, m, sg = (scr[k] for k in ("s", "rs", "xn", "l", "m", "sg"))
    v3 = v_ap.rearrange("p (n b) -> p n b", b=16)
    w = list(waits)
    t1 = pr.op("vector", "tensor_reduce", s[:, 0:nb], v3, AX.X, ALU.max,
               apply_absolute_value=True, waits=w)
    t2 = pr.op("vector", "tensor_scalar", s[:, 0:nb], s[:, 0:nb], 1.0 / 6.0, 1e-12,
               ALU.mult, ALU.max, waits=[t1])
    t3 = pr.op("vector", "reciprocal", rs[:, 0:nb], s[:, 0:nb], waits=[t2])
    rb = rs[:, 0:nb].unsqueeze(2).broadcast_to([128, nb, 16])
    t4 = pr.op("vector", "tensor_tensor", xn[:, 0:F].rearrange("p (n b) -> p n b", b=16),
               v3, rb, ALU.mult, waits=[t3])
    t5 = pr.op("vector", "tensor_tensor", xn[:, 0:F], xn[:, 0:F], xn[:, 0:F], ALU.mult,
               waits=[t4])
    wsg = w + ([scr["_last"]] if "_last" in scr else [])
    tsg = pr.op("scalar", "activation", sg[:, 0:F], v_ap, AF.Sign, waits=wsg)
    tl = pr.op("vector", "tensor_scalar", l[:, 0:F], xn[:, 0:F], 0.0625, None, ALU.is_gt,
               waits=[t5])
    for th in (0.5625, 1.5625, 3.0625):
        tl = pr.op("vector", "scalar_tensor_tensor", l[:, 0:F], xn[:, 0:F], th, l[:, 0:F],
                   ALU.is_gt, ALU.add, waits=[tl])
    tm = pr.op("vector", "tensor_scalar", m[:, 0:F], xn[:, 0:F], 6.25, None, ALU.is_gt)
    tm = pr.op("vector", "scalar_tensor_tensor", m[:, 0:F], xn[:, 0:F], 12.25, m[:, 0:F],
               ALU.is_gt, ALU.add, waits=[tm])
    tc = pr.op("vector", "scalar_tensor_tensor", m[:, 0:F], l[:, 0:F], 0.5, m[:, 0:F],
               ALU.mult, ALU.add, waits=[tl, tm])
    t7 = pr.op("vector", "tensor_scalar", l[:, 0:F], xn[:, 0:F], 25.0, None, ALU.is_gt,
               waits=[tc])
    tc = pr.op("vector", "scalar_tensor_tensor", m[:, 0:F], l[:, 0:F], 2.0, m[:, 0:F],
               ALU.mult, ALU.add, waits=[t7])
    sb3 = s[:, 0:nb].unsqueeze(2).broadcast_to([128, nb, 16])
    tq = pr.op("vector", "tensor_tensor", xn[:, 0:F].rearrange("p (n b) -> p n b", b=16),
               m[:, 0:F].rearrange("p (n b) -> p n b", b=16), sb3, ALU.mult, waits=[tc])
    tfin = pr.op("vector", "tensor_tensor", out_bf_ap, xn[:, 0:F], sg[:, 0:F], ALU.mult,
                 waits=[tq, tsg])
    scr["_last"] = tfin
    return tfin


def transpose_block(pr, dstlist_fn, src_ap_fn, nblocks, ident, tps, wait0):
    last = wait0
    for i in range(nblocks):
        tt = pr.op("tensor", "transpose", tps[i % 2][:], src_ap_fn(i), ident[:],
                   waits=[wait0, last])
        last = pr.op("vector", "tensor_copy", dstlist_fn(i), tps[i % 2][:], waits=[tt])
    return last



def build_l0_attn(nc, pr, d, ph):
    x = d["x"]
    wqT = d["wqT"]
    wkT = d["wkT"]
    wvT = d["wvT"]
    bq = d["bq"]
    bk = d["bk"]
    bv = d["bv"]
    cosq = d["cosq"]
    sinq = d["sinq"]
    eye = d["eye"]
    oTd = d["oTd"]

    es = ExitStack()
    sb = lambda nm, shape, dt=F32: es.enter_context(nc.sbuf_tensor(ph + nm, shape, dt))
    psa = lambda nm, shape, dt=F32: es.enter_context(nc.psum_tensor(ph + nm, shape, dt))

    eyef = sb("eyef", [128, 128], F32)
    eyeb = sb("eyeb", [128, 128], BF16)
    ones1 = sb("ones1", [1, 128], F32)
    t = pr.op("sync", "dma_start", out=eyef[:], in_=eye[:], dma=True)
    pr.op("vector", "tensor_copy", eyeb[:], eyef[:], waits=[t])
    pr.op("gpsimd", "memset", ones1[:], 1.0)

    PS_A = psa("PS_A", [128, 1024], F32)
    PS_B = psa("PS_B", [128, 1024], F32)
    PS_C = psa("PS_C", [128, 512], F32)

    bq_b = sb("bq_b", [128, D], BF16); bk_b = sb("bk_b", [128, D], BF16)
    bv_b = sb("bv_b", [128, D], BF16)
    es0 = ExitStack()
    bq_sb = es0.enter_context(nc.sbuf_tensor("bq_sb", [1, D], F32))
    bk_sb = es0.enter_context(nc.sbuf_tensor("bk_sb", [1, D], F32))
    bv_sb = es0.enter_context(nc.sbuf_tensor("bv_sb", [1, D], F32))
    for dst, src in ((bq_sb, bq), (bk_sb, bk), (bv_sb, bv)):
        pr.op("sync", "dma_start", out=dst[:], in_=src[:], dma=True)
    pr.barrier()
    for dstb, srow in ((bq_b, bq_sb), (bk_b, bk_sb), (bv_b, bv_sb)):
        for c0 in range(0, D, 512):
            tb = pr.op("tensor", "matmul", PS_C[:, 0:512], ones1[0:1, :],
                       srow[0:1, c0:c0 + 512], start=True, stop=True,
                       waits=[pr.tok("vector")])
            pr.op("vector", "tensor_copy", dstb[:, c0:c0 + 512], PS_C[:, 0:512], waits=[tb])
    pr.barrier()
    es0.close()

    kT_h = [sb(f"kT_h{g}", [128, T], BF16) for g in range(4)]
    qT_h = [sb(f"qT_h{g}", [128, T], BF16) for g in range(4)]
    vp = sb("vp", [128, NT * 520], BF16)
    wstg = [sb("wstg0", [128, 512], F32)]
    wstg.append(wstg[0])
    w_bf = {nm: [sb(f"{nm}_bf{k}", [128, 512], BF16) for k in range(ND)]
            for nm in ("wa", "wb")}
    wtok = sb("wtok", [128, 512], F32)
    rtmp = sb("rtmp", [128, 512], F32)
    rtmp2 = sb("rtmp2", [128, 512], F32)
    sscr = rtmp2
    wbf_tok = sb("wbf_tok", [128, 512], BF16)
    cos_t = sb("cos_t", [128, HD], F32)
    sin_t = sb("sin_t", [128, HD], F32)
    nmax = sb("nmax", [128, 128], F32)
    qn_acc = sb("qn_acc", [128, 1], F32)
    kn_acc = sb("kn_acc", [128, 1], F32)
    sqacc = sb("sqacc", [128, 1], F32)
    cval = sb("cval", [1, 4], F32)
    negc = sb("negc", [128, 1], F32)
    pr.op("gpsimd", "memset", nmax[:], 0.0)
    pr.op("gpsimd", "memset", vp[:], 0.0)
    pr.op("gpsimd", "memset",
          vp[:].rearrange("p (n j e) -> p n j e", j=8, e=65)[:, :, :, 64:65], 1.0)
    pr.barrier()

    def load_wslice(nm, dram, h0):
        last = None
        for kk in range(ND):
            b = kk % 2
            w = [last] if last else []
            ld = pr.op("sync", "dma_start", out=wstg[b][:],
                       in_=dram[kk * 128:(kk + 1) * 128, h0:h0 + 512], waits=w, dma=True)
            last = pr.op("vector", "tensor_copy", w_bf[nm][kk][:], wstg[b][:], waits=[ld])
        return last

    for half in range(2):
        h0 = half * 512
        es2 = ExitStack()
        sb2 = lambda nm, shape, dt=F32: es2.enter_context(
            nc.sbuf_tensor(f"{ph}{nm}_{half}", shape, dt))
        x0T = [sb2(f"x0T{k}", [128, T], BF16) for k in range(ND)]
        xstg = [sb2(f"xstg{i}", [128, D], F32) for i in range(2)]
        xbfs = sb2("xbfs", [128, D], BF16)
        tpsb = [es2.enter_context(nc.psum_tensor(f"{ph}tpsb{i}_{half}", [128, 128], BF16))
                for i in range(2)]
        prevt = None
        for n in range(NT):
            b = n % 2
            ld = pr.op("sync", "dma_start", out=xstg[b][:], in_=x[n * 128:(n + 1) * 128, :],
                       waits=[prevt] if prevt else [], dma=True)
            ct = pr.op("vector", "tensor_copy", xbfs[:], xstg[b][:], waits=[ld])
            prevt = transpose_block(
                pr, lambda kk, _n=n: x0T[kk][:, _n * 128:(_n + 1) * 128],
                lambda kk: xbfs[:, kk * 128:(kk + 1) * 128], ND, eyeb, tpsb, ct)
        load_wslice("wa", wkT, h0)
        load_wslice("wb", wvT, h0)
        pr.barrier()

        def proj(nm, n, bias_b, half_ps, wprev):
            ps = PS_A[:, half_ps * 512:(half_ps + 1) * 512]
            mm = None
            for kk in range(ND):
                mm = pr.op("tensor", "matmul", ps, x0T[kk][:, n * 128:(n + 1) * 128],
                           w_bf[nm][kk][:], start=(kk == 0), stop=(kk == ND - 1),
                           waits=[wprev] if (kk == 0 and wprev) else [])
            return pr.op("vector", "tensor_tensor", wtok[:], ps,
                         bias_b[:, h0:h0 + 512], ALU.add, waits=[mm])

        sqprev = [None]

        def rope(n, tev):
            lc = pr.op("sync", "dma_start", out=cos_t[:], in_=cosq[n * 128:(n + 1) * 128, :],
                       waits=[tev], dma=True)
            ls = pr.op("sync", "dma_start", out=sin_t[:], in_=sinq[n * 128:(n + 1) * 128, :],
                       waits=[tev], dma=True)
            w4 = wtok[:].rearrange("p (h j d) -> p h j d", j=2, d=32)
            u4 = rtmp2[:].rearrange("p (h j d) -> p h j d", j=2, d=32)
            sA = sin_t[:, 0:32].unsqueeze(1).broadcast_to([128, 8, 32])
            sB = sin_t[:, 32:64].unsqueeze(1).broadcast_to([128, 8, 32])
            wv1 = [tev, ls] + ([sqprev[0]] if sqprev[0] else [])
            t1 = pr.op("vector", "tensor_tensor", u4[:, :, 0, :], w4[:, :, 1, :], sA,
                       ALU.mult, waits=wv1)
            t2 = pr.op("vector", "tensor_tensor", u4[:, :, 1, :], w4[:, :, 0, :], sB,
                       ALU.mult, waits=[t1])
            cb = cos_t[:].unsqueeze(1).broadcast_to([128, 8, HD])
            t3 = pr.op("vector", "tensor_tensor",
                       rtmp[:].rearrange("p (h d) -> p h d", d=HD),
                       wtok[:].rearrange("p (h d) -> p h d", d=HD), cb, ALU.mult,
                       waits=[lc, t2])
            return pr.op("vector", "tensor_tensor", rtmp[:], rtmp[:], rtmp2[:], ALU.add,
                         waits=[t3])

        def sqnorm(acc, first, tev):
            t1 = pr.op("scalar", "activation", sscr[:], rtmp[:], AF.Square,
                       accum_out=sqacc[:], waits=[tev])
            sqprev[0] = t1
            if first:
                return pr.op("vector", "tensor_copy", acc[:], sqacc[:], waits=[t1])
            return pr.op("vector", "tensor_tensor", acc[:], acc[:], sqacc[:], ALU.max,
                         waits=[t1])

        wprev = None
        for n in range(NT):
            tev = proj("wa", n, bk_b, 0, wprev)
            tev = rope(n, tev)
            tn = sqnorm(kn_acc, n == 0, tev)
            tc = pr.op("vector", "tensor_copy", wbf_tok[:], rtmp[:], waits=[tev])
            tv = transpose_block(
                pr, lambda g, _n=n: kT_h[g][:, _n * 128:(_n + 1) * 128],
                lambda g: wbf_tok[:, g * 128:(g + 1) * 128], 4, eyeb, tpsb, tc)
            tev = proj("wb", n, bv_b, 1, tv)
            vdst = vp[:, n * 520:(n + 1) * 520].rearrange("p (j e) -> p j e", e=65)[:, :, 0:64]
            wprev = pr.op("vector", "tensor_copy", vdst,
                          wtok[:].rearrange("p (j d) -> p j d", d=64), waits=[tev])
        pr.barrier()
        load_wslice("wa", wqT, h0)
        pr.barrier()
        wprev = None
        for n in range(NT):
            tev = proj("wa", n, bq_b, 0, wprev)
            tev = rope(n, tev)
            tn = sqnorm(qn_acc, n == 0, tev)
            tc = pr.op("vector", "tensor_copy", wbf_tok[:], rtmp[:], waits=[tev])
            wprev = transpose_block(
                pr, lambda g, _n=n: qT_h[g][:, _n * 128:(_n + 1) * 128],
                lambda g: wbf_tok[:, g * 128:(g + 1) * 128], 4, eyeb, tpsb, tc)
        pr.barrier()
        es2.close()
        es3 = ExitStack()
        oT_c = [es3.enter_context(nc.sbuf_tensor(f"{ph}oT_c{g}_{half}", [128, CH], BF16))
                for g in range(4)]
        E2p = [es3.enter_context(nc.sbuf_tensor(f"{ph}E2{i}_{half}", [128, 1024], BF16))
               for i in range(2)]
        S2b = es3.enter_context(nc.psum_tensor(f"{ph}S2b_{half}", [128, 1024], F32))
        den_r = es3.enter_context(nc.sbuf_tensor(f"{ph}den_r_{half}", [1, 1024], F32))
        otmp = es3.enter_context(nc.sbuf_tensor(f"{ph}otmp_{half}", [64, 512], F32))
        tv = pr.op("vector", "tensor_copy", nmax[:, 0:1], qn_acc[:])
        tv = pr.op("vector", "tensor_copy", nmax[:, 64:65], kn_acc[:], waits=[tv])
        tq = pr.op("tensor", "transpose", PS_C[:, 0:128], nmax[:], eyef[:], waits=[tv])
        tq2 = pr.op("vector", "tensor_copy", nmax[:, 0:128], PS_C[:, 0:128], waits=[tq])
        tq2 = pr.op("vector", "tensor_reduce", cval[0:1, 0:1], nmax[0:1, :], AX.X, ALU.max,
                    waits=[tq2])
        tq2 = pr.op("vector", "tensor_reduce", cval[0:1, 1:2], nmax[64:65, :], AX.X, ALU.max,
                    waits=[tq2])
        tq2 = pr.op("vector", "tensor_tensor", cval[0:1, 2:3], cval[0:1, 0:1],
                    cval[0:1, 1:2], ALU.mult, waits=[tq2])
        tq2 = pr.op("scalar", "activation", cval[0:1, 3:4], cval[0:1, 2:3], AF.Sqrt,
                    waits=[tq2])
        tq2 = pr.op("vector", "tensor_scalar", cval[0:1, 3:4], cval[0:1, 3:4], -0.125, None,
                    ALU.mult, waits=[tq2])
        tq = pr.op("tensor", "matmul", PS_C[:, 0:1], ones1[0:1, :], cval[0:1, 3:4],
                   start=True, stop=True, waits=[tq2])
        pr.op("vector", "tensor_copy", negc[:], PS_C[:, 0:1], waits=[tq])
        pr.barrier()

        Sb = [PS_A, S2b]
        for qc in range(NCH):
            for g in range(4):
                pr.wait_only("scalar", [pr.tok("tensor")])
                avm = [None, None]
                ex_h = {}
                av_h = {}

                def s_mm(kt):
                    w = [ex_h[kt - 2]] if kt - 2 in ex_h else []
                    out = []
                    for e in range(2):
                        out.append(pr.op(
                            "tensor", "matmul", Sb[kt % 2][:, e * 512:(e + 1) * 512],
                            kT_h[g][e * 64:(e + 1) * 64, kt * 128:(kt + 1) * 128],
                            qT_h[g][e * 64:(e + 1) * 64, qc * 512:(qc + 1) * 512],
                            start=True, stop=True, waits=w, skip_group_check=True))
                    return out
                sm_h = {0: s_mm(0)}
                for kt in range(NT):
                    w = list(sm_h[kt])
                    if kt - 2 in av_h:
                        w += av_h[kt - 2]
                    ex = pr.op("scalar", "activation", E2p[kt % 2][:], Sb[kt % 2][:],
                               AF.Exp, scale=0.125, bias=negc[:, 0:1], waits=w)
                    ex_h[kt] = ex
                    if kt + 1 < NT:
                        sm_h[kt + 1] = s_mm(kt + 1)
                    avs = []
                    for e in range(2):
                        off = kt * 520 + (g * 2 + e) * 65
                        avm[e] = pr.op("tensor", "matmul", PS_B[0:65, e * 512:(e + 1) * 512],
                                       vp[:, off:off + 65],
                                       E2p[kt % 2][:, e * 512:(e + 1) * 512],
                                       start=(kt == 0), stop=(kt == NT - 1), waits=[ex],
                                       skip_group_check=True)
                        avs.append(avm[e])
                    av_h[kt] = avs
                for e in range(2):
                    dsl = den_r[0:1, e * 512:(e + 1) * 512]
                    tl = pr.op("scalar", "activation", dsl,
                               PS_B[64:65, e * 512:(e + 1) * 512], AF.Ln,
                               waits=[avm[e], pr.tok("tensor")])
                    tl = pr.op("scalar", "activation", dsl, dsl, AF.Exp,
                               scale=-1.0, waits=[tl])
                    tb = pr.op("tensor", "matmul", PS_C[0:64, 0:512], ones1[0:1, 0:64],
                               dsl, start=True, stop=True, waits=[tl])
                    tcp = pr.op("vector", "tensor_copy", otmp[:],
                                PS_B[0:64, e * 512:(e + 1) * 512], waits=[avm[e]])
                    tno = pr.op("vector", "tensor_tensor",
                                oT_c[g][e * 64:(e + 1) * 64, :], otmp[:], PS_C[0:64, 0:512],
                                ALU.mult, waits=[tb, tcp])
                    pr.wait_only("tensor", [tno])
                pr.op("gpsimd", "dma_start",
                      out=oTd[half * 4 + g, :, qc * 512:(qc + 1) * 512],
                      in_=oT_c[g][:], waits=[pr.tok("vector")], dma=True)
                pr.wait_only("tensor", [pr.tok("vector"), pr.tok("scalar")])
            pr.barrier()
        es3.close()
    pr.barrier()
    es.close()



def build_l0_wo(nc, pr, d, ph):
    x = d["x"]
    oTd = d["oTd"]
    woT = d["woT"]
    bo = d["bo"]
    y0 = d["y0"]

    es = ExitStack()
    sb = lambda nm, shape, dt=F32: es.enter_context(nc.sbuf_tensor(ph + nm, shape, dt))
    psa = lambda nm, shape, dt=F32: es.enter_context(nc.psum_tensor(ph + nm, shape, dt))

    ones1 = sb("ones1", [1, 128], F32)
    pr.op("gpsimd", "memset", ones1[:], 1.0)
    bo_sb = sb("bo_sb", [1, D])
    pr.op("sync", "dma_start", out=bo_sb[:], in_=bo[:], dma=True)
    pr.barrier()
    mps = [psa(f"mps{i}", [128, 512], F32) for i in range(2)]
    ps_bc = psa("ps_bc", [128, 512], F32)
    bo_b = sb("bo_b", [128, D])
    bcast_row(pr, ps_bc, bo_b, bo_sb, D, ones1)
    woT_bf = stage_weight_bf16(pr, sb, es, woT, ND, D, "wo")
    pr.barrier()
    oT_t = [sb(f"oT_t{k}", [128, CH], BF16) for k in range(ND)]
    res = sb("res", [128, D], F32)
    scrL = sb("scrL", [128, D], F32)
    r1 = sb("r1", [128, 1]); r2 = sb("r2", [128, 1]); musq = sb("musq", [128, 1])
    eps5 = sb("eps5", [128, 1]); pr.op("gpsimd", "memset", eps5[:], 1e-5)
    xres = [sb(f"xres{i}", [128, D], F32) for i in range(2)]
    prevx = [None, None]
    for qc in range(NCH):
        lt = []
        pr.wait_only("sync", [pr.tok("tensor")])
        for kk in range(ND):
            lt.append(pr.op("sync", "dma_start", out=oT_t[kk][:],
                            in_=oTd[kk, :, qc * 512:(qc + 1) * 512], dma=True))
        for tt4 in range(4):
            n = qc * 4 + tt4
            b = n % 2
            lx = pr.op("sync", "dma_start", out=xres[b][:], in_=x[n * 128:(n + 1) * 128, :],
                       waits=[prevx[b]] if prevx[b] else [], dma=True)
            evs = []
            for ch in range(2):
                mm = None
                for kk in range(ND):
                    mm = pr.op("tensor", "matmul", mps[ch][:],
                               oT_t[kk][:, tt4 * 128:(tt4 + 1) * 128],
                               woT_bf[kk][:, ch * 512:(ch + 1) * 512],
                               start=(kk == 0), stop=(kk == ND - 1),
                               waits=(lt + [pr.tok("vector")]) if kk == 0 else [])
                t1 = pr.op("vector", "tensor_tensor", res[:, ch * 512:(ch + 1) * 512],
                           mps[ch][:], bo_b[:, ch * 512:(ch + 1) * 512], ALU.add, waits=[mm])
                evs.append(pr.op("vector", "tensor_tensor", res[:, ch * 512:(ch + 1) * 512],
                                 res[:, ch * 512:(ch + 1) * 512],
                                 xres[b][:, ch * 512:(ch + 1) * 512], ALU.add,
                                 waits=[t1, lx]))
            tln = layernorm_tile(pr, scrL[:], res[:], scrL, r1, r2, musq, eps5, waits=evs)
            prevx[b] = tln
            pr.op("gpsimd", "dma_start", out=y0[n * 128:(n + 1) * 128, :], in_=scrL[:],
                  waits=[tln], dma=True)
            pr.wait_only("vector", [pr.tok("gpsimd")])
    pr.barrier()
    es.close()



def build_l0_ffn(nc, pr, d, ph):
    y0 = d["y0"]
    wg = d["wg"]
    wu = d["wu"]
    wd = d["wd"]
    eye = d["eye"]
    y1 = d["y1"]
    hqd = d["hqd"]

    es = ExitStack()
    sb = lambda nm, shape, dt=F32: es.enter_context(nc.sbuf_tensor(ph + nm, shape, dt))
    psa = lambda nm, shape, dt=F32: es.enter_context(nc.psum_tensor(ph + nm, shape, dt))

    eyef = sb("eyef", [128, 128], F32)
    eyeb = sb("eyeb", [128, 128], BF16)
    t = pr.op("sync", "dma_start", out=eyef[:], in_=eye[:], dma=True)
    pr.op("vector", "tensor_copy", eyeb[:], eyef[:], waits=[t])

    scr = {
        "s": sb("q_s", [128, IFF // 16], F32),
        "rs": sb("q_rs", [128, IFF // 16], F32),
        "xn": sb("q_xn", [128, IFF], F32),
        "l": sb("q_l", [128, IFF], F32),
        "m": sb("q_m", [128, IFF], F32),
        "sg": sb("q_sg", [128, IFF], F32),
    }
    tpsb = [psa(f"tpsb{i}", [128, 128], BF16) for i in range(2)]
    wstg = sb("wstg", [128, IFF], F32)
    wqbuf = sb("wqbuf", [128, IFF], BF16)
    y0f = [sb(f"y0f{i}", [128, D], F32) for i in range(2)]
    r1 = sb("r1", [128, 1]); r2 = sb("r2", [128, 1]); musq = sb("musq", [128, 1])
    eps5 = sb("eps5", [128, 1]); pr.op("gpsimd", "memset", eps5[:], 1e-5)

    def quant_transpose(dram, rows, rowlen, dstT, prevq):
        for r in range(rows):
            ld = pr.op("sync", "dma_start", out=wstg[:, 0:rowlen],
                       in_=dram[r * 128:(r + 1) * 128, :], waits=list(prevq), dma=True)
            tq = fp4_quant(pr, wqbuf[:, 0:rowlen], wstg[:, 0:rowlen], rowlen // 16,
                           scr, waits=[ld])
            last = transpose_block(
                pr, lambda c, _r=r: dstT[c][:, _r * 128:(_r + 1) * 128],
                lambda c: wqbuf[:, c * 128:(c + 1) * 128], rowlen // 128, eyeb, tpsb, tq)
            prevq = (last,)
        return prevq

    es2 = ExitStack()
    sb2 = lambda nm, shape, dt=F32: es2.enter_context(nc.sbuf_tensor(ph + nm, shape, dt))
    wgT = [sb2(f"wgT{k}", [128, IFF], BF16) for k in range(ND)]
    wuT = [sb2(f"wuT{k}", [128, IFF], BF16) for k in range(ND)]
    pq = quant_transpose(wg, NIT, D, wgT, ())
    pq = quant_transpose(wu, NIT, D, wuT, pq)
    pr.barrier()
    xqb = sb2("xqb", [128, D], BF16)
    xqT = [sb2(f"xqT{k}", [128, 128], BF16) for k in range(ND)]
    gps = psa("gps", [128, 512], F32)
    ups = psa("ups", [128, 512], F32)
    hbuf = sb2("hbuf", [128, IFF], F32)
    hqb = sb2("hqb", [128, IFF], BF16)
    prevx = [None, None]
    for n in range(NT):
        b = n % 2
        ld = pr.op("sync", "dma_start", out=y0f[b][:], in_=y0[n * 128:(n + 1) * 128, :],
                   waits=[prevx[b]] if prevx[b] else [], dma=True)
        tq = fp4_quant(pr, xqb[:], y0f[b][:], D // 16, scr, waits=[ld])
        last = transpose_block(pr, lambda kk: xqT[kk][:],
                               lambda kk: xqb[:, kk * 128:(kk + 1) * 128], ND, eyeb, tpsb, tq)
        hts = []
        for ic in range(6):
            c0 = ic * 512
            c1 = min(c0 + 512, IFF)
            cw = c1 - c0
            mmg = mmu = None
            for kk in range(ND):
                mmg = pr.op("tensor", "matmul", gps[:, 0:cw], xqT[kk][:],
                            wgT[kk][:, c0:c1], start=(kk == 0), stop=(kk == ND - 1),
                            waits=[last, pr.tok("scalar"), pr.tok("vector")] if kk == 0 else [])
            for kk in range(ND):
                mmu = pr.op("tensor", "matmul", ups[:, 0:cw], xqT[kk][:],
                            wuT[kk][:, c0:c1], start=(kk == 0), stop=(kk == ND - 1))
            ts1 = pr.op("scalar", "activation", hbuf[:, c0:c1], gps[:, 0:cw], AF.Silu,
                        waits=[mmg])
            hts.append(pr.op("vector", "tensor_tensor", hbuf[:, c0:c1], hbuf[:, c0:c1],
                             ups[:, 0:cw], ALU.mult, waits=[ts1, mmu]))
            pr.wait_only("tensor", [hts[-1]])
        tqh = fp4_quant(pr, hqb[:], hbuf[:], IFF // 16, scr, waits=hts)
        prevx[b] = tqh
        pr.op("gpsimd", "dma_start", out=hqd[n * 128:(n + 1) * 128, :], in_=hqb[:],
              waits=[tqh], dma=True)
        pr.wait_only("vector", [pr.tok("gpsimd")])
    pr.barrier()
    es2.close()

    es3 = ExitStack()
    sb3 = lambda nm, shape, dt=F32: es3.enter_context(nc.sbuf_tensor(ph + nm, shape, dt))
    wdT = [sb3(f"wdT{k}", [128, D], BF16) for k in range(NIT)]
    quant_transpose(wd, ND, IFF, wdT, ())
    pr.barrier()
    hql = [sb3(f"hql{i}", [128, IFF], BF16) for i in range(2)]
    hqT = [sb3(f"hqT{k}", [128, 128], BF16) for k in range(NIT)]
    dps = [psa(f"dps{i}", [128, 512], F32) for i in range(2)]
    res = sb3("res", [128, D], F32)
    scrL = sb3("scrL", [128, D], F32)
    prevx = [None, None]
    for n in range(NT):
        b = n % 2
        ldh = pr.op("sync", "dma_start", out=hql[b][:], in_=hqd[n * 128:(n + 1) * 128, :],
                    waits=[prevx[b]] if prevx[b] else [], dma=True)
        ldy = pr.op("sync", "dma_start", out=y0f[b][:], in_=y0[n * 128:(n + 1) * 128, :],
                    dma=True)
        last = transpose_block(pr, lambda c: hqT[c][:],
                               lambda c: hql[b][:, c * 128:(c + 1) * 128], NIT, eyeb, tpsb,
                               ldh)
        evs = []
        for ch in range(2):
            mm = None
            for c in range(NIT):
                mm = pr.op("tensor", "matmul", dps[ch][:], hqT[c][:],
                           wdT[c][:, ch * 512:(ch + 1) * 512],
                           start=(c == 0), stop=(c == NIT - 1),
                           waits=[last, pr.tok("vector")] if c == 0 else [])
            evs.append(pr.op("vector", "tensor_tensor", res[:, ch * 512:(ch + 1) * 512],
                             dps[ch][:], y0f[b][:, ch * 512:(ch + 1) * 512], ALU.add,
                             waits=[mm, ldy]))
        tln = layernorm_tile(pr, scrL[:], res[:], scrL, r1, r2, musq, eps5, waits=evs)
        prevx[b] = tln
        pr.op("gpsimd", "dma_start", out=y1[n * 128:(n + 1) * 128, :], in_=scrL[:],
              waits=[tln], dma=True)
        pr.wait_only("tensor", [tln])
        pr.wait_only("vector", [pr.tok("gpsimd")])
    pr.barrier()
    es3.close()
    es.close()



def build_l1_tm(nc, pr, d, ph):
    y1 = d["y1"]
    ln1g = d["ln1g"]
    ln1b = d["ln1b"]
    tmk = d["tmk"]
    tmv = d["tmv"]
    tmr = d["tmr"]
    tdd = d["tdd"]
    tff = d["tff"]
    wk = d["wk"]
    wv = d["wv"]
    wr = d["wr"]
    wo = d["wo"]
    eye = d["eye"]
    x2 = d["x2"]

    es = ExitStack()
    sb = lambda nm, shape, dt=F32: es.enter_context(nc.sbuf_tensor(ph + nm, shape, dt))
    psa = lambda nm, shape, dt=F32: es.enter_context(nc.psum_tensor(ph + nm, shape, dt))

    eyef = sb("eyef", [128, 128], F32)
    eyeb = sb("eyeb", [128, 128], BF16)
    ones1 = sb("ones1", [1, 128], F32)
    t = pr.op("sync", "dma_start", out=eyef[:], in_=eye[:], dma=True)
    pr.op("vector", "tensor_copy", eyeb[:], eyef[:], waits=[t])
    pr.op("gpsimd", "memset", ones1[:], 1.0)

    g_row = sb("g_row", [1, D]); b_row = sb("b_row", [1, D])
    pr.op("sync", "dma_start", out=g_row[:], in_=ln1g[:], dma=True)
    pr.op("sync", "dma_start", out=b_row[:], in_=ln1b[:], dma=True)
    mixk = sb("mixk", [128, ND]); mixv = sb("mixv", [128, ND]); mixr = sb("mixr", [128, ND])
    td_s = sb("td_s", [128, ND]); tf_s = sb("tf_s", [128, ND])
    for dst, src in ((mixk, tmk), (mixv, tmv), (mixr, tmr), (td_s, tdd), (tf_s, tff)):
        pr.op("sync", "dma_start", out=dst[:], in_=src[:], dma=True)
    pr.barrier()
    ps_bc = psa("ps_bc", [128, 512], F32)
    tpsb = [psa(f"tpsb{i}", [128, 128], BF16) for i in range(2)]
    mps = [psa(f"mps{i}", [128, 512], F32) for i in range(2)]
    g_b = sb("g_b", [128, D]); b_b = sb("b_b", [128, D])
    bcast_row(pr, ps_bc, g_b, g_row, D, ones1)
    bcast_row(pr, ps_bc, b_b, b_row, D, ones1)
    omk = sb("omk", [128, ND]); omv = sb("omv", [128, ND]); omr = sb("omr", [128, ND])
    for o, m in ((omk, mixk), (omv, mixv), (omr, mixr)):
        pr.op("vector", "tensor_scalar", o[:], m[:], -1.0, 1.0, ALU.mult, ALU.add)
    wneg = sb("wneg", [128, ND])
    te = pr.op("scalar", "activation", wneg[:], td_s[:], AF.Exp)
    pr.op("vector", "tensor_scalar", wneg[:], wneg[:], -1.0, None, ALU.mult, waits=[te])
    wstg2 = [sb(f"wstg2_{i}", [128, D], F32) for i in range(2)]
    wk_bf = stage_weight_bf16(pr, sb, es, wk, ND, D, "wk", stg=wstg2)
    wv_bf = stage_weight_bf16(pr, sb, es, wv, ND, D, "wv", stg=wstg2)
    wr_bf = stage_weight_bf16(pr, sb, es, wr, ND, D, "wr", stg=wstg2)
    wo_bf = stage_weight_bf16(pr, sb, es, wo, ND, D, "wo", stg=wstg2)
    pr.barrier()

    stA = sb("stA", [128, ND]); stB = sb("stB", [128, ND])
    stAp = sb("stAp", [128, ND]); stBp = sb("stBp", [128, ND])
    stcol = sb("stcol", [128, 1])
    hprev = sb("hprev", [128, ND], BF16)
    pr.op("gpsimd", "memset", stA[:], 0.0)
    pr.op("gpsimd", "memset", stB[:], 0.0)
    pr.op("gpsimd", "memset", hprev[:], 0.0)

    y1f = [sb(f"y1f{i}", [128, D], F32) for i in range(4)]
    h1f = sb("h1f", [128, D], F32)
    h1bf = sb("h1bf", [128, D], BF16)
    h1T = [sb(f"h1T{k}", [128, CH], BF16) for k in range(ND)]
    xkT = [sb(f"xkT{k}", [128, CH], BF16) for k in range(ND)]
    xvT = [sb(f"xvT{k}", [128, CH], BF16) for k in range(ND)]
    xrT = [sb(f"xrT{k}", [128, CH], BF16) for k in range(ND)]
    kTf = [sb(f"kTf{k}", [128, CH], F32) for k in range(ND)]
    vTf = [sb(f"vTf{k}", [128, CH], F32) for k in range(ND)]
    srT = h1T
    gT = xvT
    dfull = sb("dfull", [128, CH], F32)
    ekb = sb("ekb", [128, CH], F32)
    ekvb = sb("ekvb", [128, CH], F32)
    Ab = sb("Ab", [128, CH], F32)
    Bb = sb("Bb", [128, CH], F32)
    eub = sb("eub", [128, CH], F32)
    numb = sb("numb", [128, CH], F32)
    denb = sb("denb", [128, CH], F32)
    scrL = sb("scrL", [128, D], F32)
    x2f = sb("x2f", [128, D], F32)
    r1 = sb("r1", [128, 1]); r2 = sb("r2", [128, 1]); musq = sb("musq", [128, 1])
    eps3 = sb("eps3", [128, 1]); pr.op("gpsimd", "memset", eps3[:], 1e-3)

    for qc in range(NCH):
        lds = []
        pr.wait_only("sync", [pr.tok("gpsimd")])
        for tt4 in range(4):
            n = qc * 4 + tt4
            lds.append(pr.op("sync", "dma_start", out=y1f[tt4][:],
                             in_=y1[n * 128:(n + 1) * 128, :], dma=True))
        for tt4 in range(4):
            tln = layernorm_tile(pr, h1f[:], y1f[tt4][:], scrL, r1, r2, musq, eps3,
                                 gain_b=g_b, bias_b=b_b, waits=[lds[tt4]])
            tc = pr.op("vector", "tensor_copy", h1bf[:], h1f[:], waits=[tln])
            transpose_block(pr, lambda kk, _t=tt4: h1T[kk][:, _t * 128:(_t + 1) * 128],
                            lambda kk: h1bf[:, kk * 128:(kk + 1) * 128], ND, eyeb, tpsb, tc)
        pr.barrier()
        for kk in range(ND):
            for dst, mx, om in ((xkT, mixk, omk), (xvT, mixv, omv), (xrT, mixr, omr)):
                t1 = pr.op("vector", "tensor_scalar", dst[kk][:], h1T[kk][:],
                           mx[:, kk:kk + 1], None, ALU.mult)
                t2 = pr.op("vector", "scalar_tensor_tensor", dst[kk][:, 1:CH],
                           h1T[kk][:, 0:CH - 1], om[:, kk:kk + 1], dst[kk][:, 1:CH],
                           ALU.mult, ALU.add, waits=[t1])
                pr.op("vector", "scalar_tensor_tensor", dst[kk][:, 0:1],
                      hprev[:, kk:kk + 1], om[:, kk:kk + 1], dst[kk][:, 0:1],
                      ALU.mult, ALU.add, waits=[t2])
            pr.op("vector", "tensor_copy", hprev[:, kk:kk + 1], h1T[kk][:, CH - 1:CH])
        pr.barrier()
        for m in range(ND):
            mm = None
            for kk in range(ND):
                mm = pr.op("tensor", "matmul", mps[0][:], wk_bf[kk][:, m * 128:(m + 1) * 128],
                           xkT[kk][:], start=(kk == 0), stop=(kk == ND - 1),
                           waits=[pr.tok("vector")] if kk == 0 else [])
            tk = pr.op("vector", "tensor_copy", kTf[m][:], mps[0][:], waits=[mm])
            mm = None
            for kk in range(ND):
                mm = pr.op("tensor", "matmul", mps[1][:], wv_bf[kk][:, m * 128:(m + 1) * 128],
                           xvT[kk][:], start=(kk == 0), stop=(kk == ND - 1),
                           waits=[tk] if kk == 0 else [])
            tv = pr.op("vector", "tensor_copy", vTf[m][:], mps[1][:], waits=[mm])
            mm = None
            for kk in range(ND):
                mm = pr.op("tensor", "matmul", mps[0][:], wr_bf[kk][:, m * 128:(m + 1) * 128],
                           xrT[kk][:], start=(kk == 0), stop=(kk == ND - 1),
                           waits=[tv] if kk == 0 else [])
            ts1 = pr.op("scalar", "activation", srT[m][:], mps[0][:], AF.Sigmoid, waits=[mm])
            pr.wait_only("tensor", [ts1])
        pr.barrier()
        for m in range(ND):
            td = pr.op("scalar", "activation", dfull[:], g_b[:, 0:CH], AF.Exp, scale=0.0,
                       bias=wneg[:, m:m + 1], waits=[pr.tok("vector")])
            t0 = pr.op("vector", "tensor_copy", stAp[:, m:m + 1], stA[:, m:m + 1])
            t0b = pr.op("vector", "tensor_copy", stBp[:, m:m + 1], stB[:, m:m + 1])
            te = pr.op("scalar", "activation", ekb[:], kTf[m][:], AF.Exp, waits=[td])
            tev = pr.op("vector", "tensor_tensor", ekvb[:], ekb[:], vTf[m][:], ALU.mult,
                        waits=[te])
            tf1 = pr.op("vector", "tensor_tensor", stcol[:], dfull[:, 0:1], stAp[:, m:m + 1],
                        ALU.mult, waits=[td, t0])
            tf1 = pr.op("vector", "tensor_tensor", ekvb[:, 0:1], ekvb[:, 0:1], stcol[:],
                        ALU.add, waits=[tf1, tev])
            tA = pr.op("vector", "tensor_tensor_scan", Ab[:], dfull[:], ekvb[:],
                       0.0, ALU.mult, ALU.add, waits=[tf1])
            tsA = pr.op("vector", "tensor_copy", stA[:, m:m + 1], Ab[:, CH - 1:CH], waits=[tA])
            tf2 = pr.op("vector", "tensor_tensor", stcol[:], dfull[:, 0:1], stBp[:, m:m + 1],
                        ALU.mult, waits=[t0b])
            tf2 = pr.op("vector", "tensor_tensor", ekb[:, 0:1], ekb[:, 0:1], stcol[:],
                        ALU.add, waits=[tf2])
            tB = pr.op("vector", "tensor_tensor_scan", Bb[:], dfull[:], ekb[:],
                       0.0, ALU.mult, ALU.add, waits=[tf2])
            tsB = pr.op("vector", "tensor_copy", stB[:, m:m + 1], Bb[:, CH - 1:CH], waits=[tB])
            tu = pr.op("scalar", "activation", eub[:], kTf[m][:], AF.Exp,
                       bias=tf_s[:, m:m + 1], waits=[te])
            tn = pr.op("vector", "tensor_tensor", numb[:], eub[:], vTf[m][:], ALU.mult,
                       waits=[tu])
            tn = pr.op("vector", "tensor_tensor", numb[:, 1:CH], numb[:, 1:CH],
                       Ab[:, 0:CH - 1], ALU.add, waits=[tn, tA])
            tn = pr.op("vector", "tensor_tensor", numb[:, 0:1], numb[:, 0:1],
                       stAp[:, m:m + 1], ALU.add, waits=[tn])
            td1 = pr.op("vector", "tensor_tensor", denb[:, 1:CH], eub[:, 1:CH],
                        Bb[:, 0:CH - 1], ALU.add, waits=[tB, tu])
            td1 = pr.op("vector", "tensor_tensor", denb[:, 0:1], eub[:, 0:1],
                        stBp[:, m:m + 1], ALU.add, waits=[td1])
            tl = pr.op("scalar", "activation", denb[:], denb[:], AF.Ln, waits=[td1])
            tl = pr.op("scalar", "activation", denb[:], denb[:], AF.Exp, scale=-1.0,
                       waits=[tl])
            tw = pr.op("vector", "tensor_tensor", numb[:], numb[:], denb[:], ALU.mult,
                       waits=[tn, tl])
            pr.op("vector", "tensor_tensor", gT[m][:], numb[:], srT[m][:], ALU.mult,
                  waits=[tw])
        pr.barrier()
        for tt4 in range(4):
            n = qc * 4 + tt4
            evs = []
            for ch in range(2):
                mm = None
                for m in range(ND):
                    mm = pr.op("tensor", "matmul", mps[ch][:],
                               gT[m][:, tt4 * 128:(tt4 + 1) * 128],
                               wo_bf[m][:, ch * 512:(ch + 1) * 512],
                               start=(m == 0), stop=(m == ND - 1),
                               waits=[pr.tok("vector")] if m == 0 else [])
                evs.append(pr.op("vector", "tensor_tensor", x2f[:, ch * 512:(ch + 1) * 512],
                                 mps[ch][:], y1f[tt4][:, ch * 512:(ch + 1) * 512], ALU.add,
                                 waits=[mm]))
            st = pr.op("gpsimd", "dma_start", out=x2[n * 128:(n + 1) * 128, :], in_=x2f[:],
                       waits=evs, dma=True)
            pr.wait_only("vector", [st])
            pr.wait_only("tensor", evs)
        pr.barrier()
    pr.barrier()
    es.close()



def build_l1_cma(nc, pr, d, ph):
    x2 = d["x2"]
    ln2g = d["ln2g"]
    ln2b = d["ln2b"]
    cmk = d["cmk"]
    cmr = d["cmr"]
    wkc = d["wkc"]
    wrc = d["wrc"]
    eye = d["eye"]
    kcd = d["kcd"]
    rcd = d["rcd"]

    es = ExitStack()
    sb = lambda nm, shape, dt=F32: es.enter_context(nc.sbuf_tensor(ph + nm, shape, dt))
    psa = lambda nm, shape, dt=F32: es.enter_context(nc.psum_tensor(ph + nm, shape, dt))

    eyef = sb("eyef", [128, 128], F32)
    eyeb = sb("eyeb", [128, 128], BF16)
    ones1 = sb("ones1", [1, 128], F32)
    t = pr.op("sync", "dma_start", out=eyef[:], in_=eye[:], dma=True)
    pr.op("vector", "tensor_copy", eyeb[:], eyef[:], waits=[t])
    pr.op("gpsimd", "memset", ones1[:], 1.0)
    g_row = sb("g_row", [1, D]); b_row = sb("b_row", [1, D])
    pr.op("sync", "dma_start", out=g_row[:], in_=ln2g[:], dma=True)
    pr.op("sync", "dma_start", out=b_row[:], in_=ln2b[:], dma=True)
    mixk = sb("mixk", [128, ND]); mixr = sb("mixr", [128, ND])
    pr.op("sync", "dma_start", out=mixk[:], in_=cmk[:], dma=True)
    pr.op("sync", "dma_start", out=mixr[:], in_=cmr[:], dma=True)
    pr.barrier()
    ps_bc = psa("ps_bc", [128, 512], F32)
    tpsb = [psa(f"tpsb{i}", [128, 128], BF16) for i in range(2)]
    mps = [psa(f"mps{i}", [128, 512], F32) for i in range(2)]
    g_b = sb("g_b", [128, D]); b_b = sb("b_b", [128, D])
    bcast_row(pr, ps_bc, g_b, g_row, D, ones1)
    bcast_row(pr, ps_bc, b_b, b_row, D, ones1)
    omk = sb("omk", [128, ND]); omr = sb("omr", [128, ND])
    for o, m in ((omk, mixk), (omr, mixr)):
        pr.op("vector", "tensor_scalar", o[:], m[:], -1.0, 1.0, ALU.mult, ALU.add)
    wkc_bf = stage_weight_bf16(pr, sb, es, wkc, ND, 4096, "wkc")
    wrc_bf = stage_weight_bf16(pr, sb, es, wrc, ND, D, "wrc")
    pr.barrier()

    hprev = sb("hprev", [128, ND], BF16)
    pr.op("gpsimd", "memset", hprev[:], 0.0)
    x2f = [sb(f"x2f{i}", [128, D], F32) for i in range(4)]
    h1f = sb("h1f", [128, D], F32)
    h1bf = sb("h1bf", [128, D], BF16)
    h1T = [sb(f"h1T{k}", [128, CH], BF16) for k in range(ND)]
    xkT = [sb(f"xkT{k}", [128, CH], BF16) for k in range(ND)]
    xrT = [sb(f"xrT{k}", [128, CH], BF16) for k in range(ND)]
    kcb = sb("kcb", [128, CH], BF16)
    rcb = sb("rcb", [128, D], BF16)
    relu_b = sb("relu_b", [128, CH], F32)
    scrL = sb("scrL", [128, D], F32)
    r1 = sb("r1", [128, 1]); r2 = sb("r2", [128, 1]); musq = sb("musq", [128, 1])
    eps3 = sb("eps3", [128, 1]); pr.op("gpsimd", "memset", eps3[:], 1e-3)

    for qc in range(NCH):
        pr.wait_only("sync", [pr.tok("gpsimd")])
        lds = []
        for tt4 in range(4):
            n = qc * 4 + tt4
            lds.append(pr.op("sync", "dma_start", out=x2f[tt4][:],
                             in_=x2[n * 128:(n + 1) * 128, :], dma=True))
        for tt4 in range(4):
            tln = layernorm_tile(pr, h1f[:], x2f[tt4][:], scrL, r1, r2, musq, eps3,
                                 gain_b=g_b, bias_b=b_b, waits=[lds[tt4]])
            tc = pr.op("vector", "tensor_copy", h1bf[:], h1f[:], waits=[tln])
            transpose_block(pr, lambda kk, _t=tt4: h1T[kk][:, _t * 128:(_t + 1) * 128],
                            lambda kk: h1bf[:, kk * 128:(kk + 1) * 128], ND, eyeb, tpsb, tc)
        pr.barrier()
        for kk in range(ND):
            for dst, mx, om in ((xkT, mixk, omk), (xrT, mixr, omr)):
                t1 = pr.op("vector", "tensor_scalar", dst[kk][:], h1T[kk][:],
                           mx[:, kk:kk + 1], None, ALU.mult)
                t2 = pr.op("vector", "scalar_tensor_tensor", dst[kk][:, 1:CH],
                           h1T[kk][:, 0:CH - 1], om[:, kk:kk + 1], dst[kk][:, 1:CH],
                           ALU.mult, ALU.add, waits=[t1])
                pr.op("vector", "scalar_tensor_tensor", dst[kk][:, 0:1],
                      hprev[:, kk:kk + 1], om[:, kk:kk + 1], dst[kk][:, 0:1],
                      ALU.mult, ALU.add, waits=[t2])
            pr.op("vector", "tensor_copy", hprev[:, kk:kk + 1], h1T[kk][:, CH - 1:CH])
        pr.barrier()
        for tt4 in range(4):
            n = qc * 4 + tt4
            for ch in range(2):
                mm = None
                for kk in range(ND):
                    mm = pr.op("tensor", "matmul", mps[ch][:],
                               xrT[kk][:, tt4 * 128:(tt4 + 1) * 128],
                               wrc_bf[kk][:, ch * 512:(ch + 1) * 512],
                               start=(kk == 0), stop=(kk == ND - 1),
                               waits=[pr.tok("gpsimd"), pr.tok("vector")] if kk == 0 else [])
                ts1 = pr.op("scalar", "activation", rcb[:, ch * 512:(ch + 1) * 512],
                            mps[ch][:], AF.Sigmoid, waits=[mm])
                pr.wait_only("tensor", [ts1])
            pr.op("gpsimd", "dma_start", out=rcd[n * 128:(n + 1) * 128, :], in_=rcb[:],
                  waits=[pr.tok("scalar")], dma=True)
        for m in range(32):
            mm = None
            for kk in range(ND):
                mm = pr.op("tensor", "matmul", mps[m % 2][:],
                           wkc_bf[kk][:, m * 128:(m + 1) * 128], xkT[kk][:],
                           start=(kk == 0), stop=(kk == ND - 1),
                           waits=[pr.tok("scalar"), pr.tok("gpsimd")] if kk == 0 else [])
            tr = pr.op("vector", "tensor_scalar", relu_b[:], mps[m % 2][:], 0.0, None,
                       ALU.max, waits=[mm])
            ts1 = pr.op("scalar", "activation", kcb[:], relu_b[:], AF.Square, waits=[tr])
            pr.op("gpsimd", "dma_start", out=kcd[m, :, qc * 512:(qc + 1) * 512], in_=kcb[:],
                  waits=[ts1], dma=True)
            pr.wait_only("tensor", [tr])
            pr.wait_only("scalar", [pr.tok("gpsimd")])
        pr.barrier()
    pr.barrier()
    es.close()



def build_l1_cmb(nc, pr, d, ph):
    x2 = d["x2"]
    y1 = d["y1"]
    kcd = d["kcd"]
    rcd = d["rcd"]
    wvc = d["wvc"]
    y2 = d["y2"]

    es = ExitStack()
    sb = lambda nm, shape, dt=F32: es.enter_context(nc.sbuf_tensor(ph + nm, shape, dt))
    psa = lambda nm, shape, dt=F32: es.enter_context(nc.psum_tensor(ph + nm, shape, dt))

    mps = [psa(f"mps{i}", [128, 512], F32) for i in range(2)]
    wvc_bf = stage_weight_bf16(pr, sb, es, wvc, 32, D, "wvc")
    pr.barrier()
    kct = [sb(f"kct{m}", [128, CH], BF16) for m in range(32)]
    rcf = [sb(f"rcf{i}", [128, D], BF16) for i in range(4)]
    x2f = [sb(f"x2f{i}", [128, D], F32) for i in range(4)]
    y1f = sb("y1f", [128, D], F32)
    outf = sb("outf", [128, D], F32)
    scrL = sb("scrL", [128, D], F32)
    r1 = sb("r1", [128, 1]); r2 = sb("r2", [128, 1]); musq = sb("musq", [128, 1])
    eps5 = sb("eps5", [128, 1]); pr.op("gpsimd", "memset", eps5[:], 1e-5)

    for qc in range(NCH):
        pr.wait_only("sync", [pr.tok("tensor"), pr.tok("vector")])
        lk = []
        for m in range(32):
            lk.append(pr.op("sync", "dma_start", out=kct[m][:],
                            in_=kcd[m, :, qc * 512:(qc + 1) * 512], dma=True))
        for tt4 in range(4):
            n = qc * 4 + tt4
            lds = [pr.op("sync", "dma_start", out=rcf[tt4][:],
                         in_=rcd[n * 128:(n + 1) * 128, :], dma=True),
                   pr.op("sync", "dma_start", out=x2f[tt4][:],
                         in_=x2[n * 128:(n + 1) * 128, :], dma=True)]
        for tt4 in range(4):
            n = qc * 4 + tt4
            ly = pr.op("sync", "dma_start", out=y1f[:], in_=y1[n * 128:(n + 1) * 128, :],
                       waits=[pr.tok("vector")], dma=True)
            txy = pr.op("vector", "tensor_tensor", x2f[tt4][:], x2f[tt4][:], y1f[:],
                        ALU.add, waits=[ly] + lk)
            evs = []
            for ch in range(2):
                mm = None
                for m in range(32):
                    mm = pr.op("tensor", "matmul", mps[ch][:],
                               kct[m][:, tt4 * 128:(tt4 + 1) * 128],
                               wvc_bf[m][:, ch * 512:(ch + 1) * 512],
                               start=(m == 0), stop=(m == 31),
                               waits=[txy] if m == 0 else [])
                t1 = pr.op("vector", "tensor_tensor", outf[:, ch * 512:(ch + 1) * 512],
                           mps[ch][:], rcf[tt4][:, ch * 512:(ch + 1) * 512], ALU.mult,
                           waits=[mm])
                evs.append(pr.op("vector", "tensor_tensor", outf[:, ch * 512:(ch + 1) * 512],
                                 outf[:, ch * 512:(ch + 1) * 512], x2f[tt4][:, ch * 512:(ch + 1) * 512],
                                 ALU.add, waits=[t1]))
            tln = layernorm_tile(pr, scrL[:], outf[:], scrL, r1, r2, musq, eps5, waits=evs)
            pr.op("gpsimd", "dma_start", out=y2[n * 128:(n + 1) * 128, :], in_=scrL[:],
                  waits=[tln], dma=True)
            pr.wait_only("tensor", [tln])
            pr.wait_only("vector", [pr.tok("gpsimd")])
    pr.barrier()
    es.close()



def build_l1_ffn(nc, pr, d, ph):
    y2 = d["y2"]
    wT = d["wT"]
    alpha = d["alpha"]
    bias = d["bias"]
    eye = d["eye"]
    y = d["y"]

    es = ExitStack()
    sb = lambda nm, shape, dt=F32: es.enter_context(nc.sbuf_tensor(ph + nm, shape, dt))
    psa = lambda nm, shape, dt=F32: es.enter_context(nc.psum_tensor(ph + nm, shape, dt))

    eyef = sb("eyef", [128, 128], F32)
    eyeb = sb("eyeb", [128, 128], BF16)
    ones1 = sb("ones1", [1, 128], F32)
    onecol = sb("onecol", [128, 1], F32)
    t = pr.op("sync", "dma_start", out=eyef[:], in_=eye[:], dma=True)
    pr.op("vector", "tensor_copy", eyeb[:], eyef[:], waits=[t])
    pr.op("gpsimd", "memset", ones1[:], 1.0)
    pr.op("gpsimd", "memset", onecol[:], 1.0)
    b_row = sb("b_row", [1, D])
    al_row = sb("al_row", [1, 1])
    pr.op("sync", "dma_start", out=b_row[:], in_=bias[:], dma=True)
    pr.op("sync", "dma_start", out=al_row[:], in_=alpha[:], dma=True)
    wf = [sb(f"wf{k}", [128, D], F32) for k in range(ND)]
    for k in range(ND):
        pr.op("sync", "dma_start", out=wf[k][:], in_=wT[k * 128:(k + 1) * 128, :], dma=True)
    pr.barrier()
    ps_bc = psa("ps_bc", [128, 512], F32)
    tpsb = [psa(f"tpsb{i}", [128, 128], BF16) for i in range(2)]
    mps = [psa(f"mps{i}", [128, 512], F32) for i in range(2)]
    bias_b = sb("bias_b", [128, D], F32)
    bcast_row(pr, ps_bc, bias_b, b_row, D, ones1)
    colsum = sb("colsum", [128, ND], F32)
    sc1 = sb("sc1", [128, 1], F32)
    dval = sb("dval", [1, 2], F32)
    for k in range(ND):
        pr.op("vector", "tensor_reduce", colsum[:, k:k + 1], wf[k][:], AX.X, ALU.add,
              apply_absolute_value=True)
    tv = pr.op("vector", "tensor_reduce", sc1[:], colsum[:], AX.X, ALU.add)
    tm = pr.op("tensor", "matmul", ps_bc[0:1, 0:1], onecol[:], sc1[:], start=True, stop=True,
               waits=[tv])
    tv = pr.op("vector", "tensor_scalar", dval[0:1, 0:1], ps_bc[0:1, 0:1], 0.7 / (D * D),
               None, ALU.mult, waits=[tm])
    tb = pr.op("tensor", "matmul", ps_bc[:, 1:2], ones1[0:1, :], dval[0:1, 0:1],
               start=True, stop=True, waits=[tv])
    delta_c = sb("delta_c", [128, 1], F32)
    tv = pr.op("vector", "tensor_copy", delta_c[:], ps_bc[:, 1:2], waits=[tb])
    ta = pr.op("tensor", "matmul", ps_bc[:, 2:3], ones1[0:1, :], al_row[0:1, :],
               start=True, stop=True, waits=[tv])
    alpha_c = sb("alpha_c", [128, 1], F32)
    pr.op("vector", "tensor_copy", alpha_c[:], ps_bc[:, 2:3], waits=[ta])
    pr.barrier()
    wqb = [sb(f"wqb{k}", [128, D], BF16) for k in range(ND)]
    sgb = sb("sgb", [128, D], F32)
    mb = sb("mb", [128, D], F32)
    dsq = sb("dsq", [128, 1], F32)
    pr.op("vector", "tensor_tensor", dsq[:], delta_c[:], delta_c[:], ALU.mult)
    for k in range(ND):
        ts1 = pr.op("scalar", "activation", sgb[:], wf[k][:], AF.Sign,
                    waits=[pr.tok("vector")])
        tv = pr.op("vector", "tensor_tensor", mb[:], wf[k][:], wf[k][:], ALU.mult)
        tv = pr.op("vector", "tensor_scalar", mb[:], mb[:], dsq[:, 0:1], None, ALU.is_gt,
                   waits=[tv])
        pr.op("vector", "tensor_tensor", wqb[k][:], mb[:], sgb[:], ALU.mult,
              waits=[ts1, tv])
    pr.barrier()
    y2f = [sb(f"y2f{i}", [128, D], F32) for i in range(2)]
    xb = sb("xb", [128, D], BF16)
    xT = [sb(f"xT{k}", [128, 128], BF16) for k in range(ND)]
    res = sb("res", [128, D], F32)
    scrL = sb("scrL", [128, D], F32)
    r1 = sb("r1", [128, 1]); r2 = sb("r2", [128, 1]); musq = sb("musq", [128, 1])
    eps5 = sb("eps5", [128, 1]); pr.op("gpsimd", "memset", eps5[:], 1e-5)
    prevx = [None, None]
    for n in range(NT):
        b = n % 2
        ld = pr.op("sync", "dma_start", out=y2f[b][:], in_=y2[n * 128:(n + 1) * 128, :],
                   waits=[prevx[b]] if prevx[b] else [], dma=True)
        tc = pr.op("vector", "tensor_copy", xb[:], y2f[b][:], waits=[ld])
        last = transpose_block(pr, lambda kk: xT[kk][:],
                               lambda kk: xb[:, kk * 128:(kk + 1) * 128], ND, eyeb, tpsb, tc)
        evs = []
        for ch in range(2):
            mm = None
            for kk in range(ND):
                mm = pr.op("tensor", "matmul", mps[ch][:], xT[kk][:],
                           wqb[kk][:, ch * 512:(ch + 1) * 512],
                           start=(kk == 0), stop=(kk == ND - 1),
                           waits=[last, pr.tok("vector")] if kk == 0 else [])
            t1 = pr.op("vector", "scalar_tensor_tensor", res[:, ch * 512:(ch + 1) * 512],
                       mps[ch][:], alpha_c[:, 0:1], bias_b[:, ch * 512:(ch + 1) * 512],
                       ALU.mult, ALU.add, waits=[mm])
            evs.append(pr.op("vector", "tensor_tensor", res[:, ch * 512:(ch + 1) * 512],
                             res[:, ch * 512:(ch + 1) * 512],
                             y2f[b][:, ch * 512:(ch + 1) * 512], ALU.add, waits=[t1]))
        tln = layernorm_tile(pr, scrL[:], res[:], scrL, r1, r2, musq, eps5, waits=evs)
        prevx[b] = tln
        pr.op("gpsimd", "dma_start", out=y[n * 128:(n + 1) * 128, :], in_=scrL[:],
              waits=[tln], dma=True)
        pr.wait_only("tensor", [tln])
        pr.wait_only("vector", [pr.tok("gpsimd")])
    pr.barrier()
    es.close()



def _rope_tables():
    inv = (1.0 / (np.float32(10000.0) ** (np.arange(0, HD, 2, dtype=np.float32) / np.float32(HD)))).astype(np.float32)
    tpos = np.arange(T, dtype=np.float32)[:, None]
    freqs = (tpos * inv[None, :]).astype(np.float32)
    emb = np.concatenate([freqs, freqs], axis=-1)
    cos = np.cos(emb).astype(np.float32)
    sin = np.sin(emb).astype(np.float32)
    sinf = sin.copy()
    sinf[:, 0:32] = -sinf[:, 0:32]
    return cos, sinf


def _cm(v):
    return np.ascontiguousarray(np.asarray(v, np.float32).reshape(ND, 128).T)


_NC_CACHE = {}


def _get(name, fn):
    if name not in _NC_CACHE:
        _NC_CACHE[name] = fn()
    return _NC_CACHE[name]


def build_fused():
    nc = bass.Bass("TRN2", target_bir_lowering=False)
    EI = lambda nm, shape, dt=F32: nc.dram_tensor(nm, shape, dt, kind="ExternalInput").ap()
    IN = lambda nm, shape, dt=F32: nc.dram_tensor(nm, shape, dt, kind="Internal").ap()
    d = {}
    d["x"] = EI("x", [T, D])
    for nm in ("wqT", "wkT", "wvT", "woT", "wk", "wv", "wr", "wo", "wrc", "wT"):
        d[nm] = EI(nm, [D, D])
    for nm in ("bq", "bk", "bv", "bo", "ln1g", "ln1b", "ln2g", "ln2b", "bias"):
        d[nm] = EI(nm, [1, D])
    for nm in ("tmk", "tmv", "tmr", "tdd", "tff", "cmk", "cmr"):
        d[nm] = EI(nm, [128, ND])
    d["cosq"] = EI("cosq", [T, HD])
    d["sinq"] = EI("sinq", [T, HD])
    d["eye"] = EI("eye", [128, 128])
    d["wg"] = EI("wg", [IFF, D])
    d["wu"] = EI("wu", [IFF, D])
    d["wd"] = EI("wd", [D, IFF])
    d["wkc"] = EI("wkc", [D, 4096])
    d["wvc"] = EI("wvc", [4096, D])
    d["alpha"] = EI("alpha", [1, 1])
    d["oTd"] = IN("oTd", [ND, 128, T], BF16)
    d["hqd"] = IN("hqd", [T, IFF], BF16)
    d["kcd"] = IN("kcd", [32, 128, T], BF16)
    d["rcd"] = IN("rcd", [T, D], BF16)
    for nm in ("y0", "y1", "x2", "y2"):
        d[nm] = IN(nm, [T, D])
    d["y"] = nc.dram_tensor("y", [T, D], F32, kind="ExternalOutput").ap()

    es = ExitStack()
    pr = Pr(nc, es)
    build_l0_attn(nc, pr, d, "A_")
    build_l0_wo(nc, pr, dict(d, y0=d["y0"]), "B_")
    build_l0_ffn(nc, pr, d, "C_")
    build_l1_tm(nc, pr, dict(d, y1=d["y1"]), "D_")
    build_l1_cma(nc, pr, d, "E_")
    build_l1_cmb(nc, pr, d, "F_")
    build_l1_ffn(nc, pr, dict(d, y2=d["y2"]), "G_")
    pr.finalize()
    es.close()
    return nc


def kernel(**inp):
    f = lambda k: np.ascontiguousarray(np.asarray(inp[k], np.float32))
    x = f("L0_attn_x")
    B = x.shape[0]
    eye = np.eye(128, dtype=np.float32)
    cos, sinf = _rope_tables()
    row = lambda v: np.ascontiguousarray(np.asarray(v, np.float32).reshape(1, -1))
    nc = _get("fused", build_fused)
    common = dict(
        wqT=np.ascontiguousarray(f("L0_attn_Wq").T),
        wkT=np.ascontiguousarray(f("L0_attn_Wk").T),
        wvT=np.ascontiguousarray(f("L0_attn_Wv").T),
        woT=np.ascontiguousarray(f("L0_attn_Wo").T),
        bq=row(f("L0_attn_bq")), bk=row(f("L0_attn_bk")), bv=row(f("L0_attn_bv")),
        bo=row(f("L0_attn_bo")), cosq=cos, sinq=sinf, eye=eye,
        wg=f("L0_ffn_w_gate"), wu=f("L0_ffn_w_up"), wd=f("L0_ffn_w_down"),
        ln1g=row(f("L1_attn_ln1_g")), ln1b=row(f("L1_attn_ln1_b")),
        ln2g=row(f("L1_attn_ln2_g")), ln2b=row(f("L1_attn_ln2_b")),
        tmk=_cm(f("L1_attn_tm_mix_k")), tmv=_cm(f("L1_attn_tm_mix_v")),
        tmr=_cm(f("L1_attn_tm_mix_r")), tdd=_cm(f("L1_attn_time_decay")),
        tff=_cm(f("L1_attn_time_first")),
        wk=f("L1_attn_tm_Wk"), wv=f("L1_attn_tm_Wv"), wr=f("L1_attn_tm_Wr"),
        wo=f("L1_attn_Wo"),
        cmk=_cm(f("L1_attn_cm_mix_k")), cmr=_cm(f("L1_attn_cm_mix_r")),
        wkc=f("L1_attn_cm_Wk"), wvc=f("L1_attn_cm_Wv"), wrc=f("L1_attn_cm_Wr"),
        wT=np.ascontiguousarray(f("L1_ffn_weight").T),
        alpha=f("L1_ffn_alpha").reshape(1, 1), bias=row(f("L1_ffn_bias")))
    maps = [dict(common, x=np.ascontiguousarray(x[b])) for b in range(B)]
    r = run_bass_kernel_spmd(nc, maps, core_ids=list(range(B)))
    return np.stack([np.asarray(q["y"], np.float32) for q in r.results], axis=0)
